# version 22
# baseline (speedup 1.0000x reference)
import numpy as np
import ml_dtypes
from contextlib import ExitStack

import concourse.bass as bass
import concourse.mybir as mybir
from concourse.bass_utils import run_bass_kernel_spmd

F32 = mybir.dt.float32
BF16 = mybir.dt.bfloat16
AF = mybir.ActivationFunctionType
ALU = mybir.AluOpType
AX = mybir.AxisListType
BF = ml_dtypes.bfloat16

S = 4096
DM = 2048
NT = 8
HF = 8192
NJ = 64
SSOFT = float(1.0 / np.sqrt(np.float32(2048.0)))
MAGIC = float(np.float32(12582912.0))
LNEPS = 1e-5
RMSEPS = float(np.finfo(np.float32).eps)

ENGS = ("T", "V", "S", "G", "Y")


class Ctx:
    def __init__(self, nc):
        self.nc = nc
        self.ops = []
        self.cnt = {}
        self.sem = {}
        self.ewait = {e: {} for e in ENGS}

    def newsem(self, stack, name):
        self.sem[name] = stack.enter_context(self.nc.semaphore(name))
        self.cnt[name] = 0

    def op(self, eng, fn, deps=(), inc=None, incn=1):
        dd = []
        cache = self.ewait[eng]
        for d in deps:
            if d is None:
                continue
            sn, val = d
            if val is None or val <= 0:
                continue
            if cache.get(sn, 0) >= val:
                continue
            cache[sn] = val
            dd.append((sn, val))
        ret = None
        if inc is not None:
            self.cnt[inc] += incn
            ret = self.cnt[inc]
        sems = self.sem

        def f(e, _dd=tuple(dd), _fn=fn, _inc=inc, _incn=incn):
            for sn, val in _dd:
                e.wait_ge(sems[sn], val)
            inst = _fn(e)
            if _inc is not None:
                inst.then_inc(sems[_inc], _incn)
        self.ops.append((eng, f))
        return ret

    def barrier(self, extra=()):
        snap = [(n, self.cnt[n]) for n in ("t", "v", "s", "g")]
        snap += [x for x in extra if x is not None]
        for eng in ENGS:
            self.op(eng, lambda e: None, deps=snap)


def build_program(debug=()):
    nc = bass.Bass("TRN2", target_bir_lowering=False)

    x_in = nc.dram_tensor("x", [S, DM], F32, kind="ExternalInput").ap()
    qkv_w = nc.dram_tensor("qkv", [DM, 3 * DM], F32, kind="ExternalInput").ap()
    o_w = nc.dram_tensor("o_w", [DM, DM], F32, kind="ExternalInput").ap()
    gbc_in = nc.dram_tensor("gbc", [128, DM], F32, kind="ExternalInput").ap()
    bbc_in = nc.dram_tensor("bbc", [128, DM], F32, kind="ExternalInput").ap()
    wgu_in = nc.dram_tensor("wgu", [2 * HF, DM], F32, kind="ExternalInput").ap()
    wp_in = nc.dram_tensor("wp", [DM, HF], F32, kind="ExternalInput").ap()
    ident_in = nc.dram_tensor("ident", [128, 128], BF16, kind="ExternalInput").ap()
    negm_in = nc.dram_tensor("negm", [128, 128], F32, kind="ExternalInput").ap()
    ones_in = nc.dram_tensor("onesb", [128, 1], BF16, kind="ExternalInput").ap()
    onef_in = nc.dram_tensor("onef", [1, 1], F32, kind="ExternalInput").ap()
    y_out = nc.dram_tensor("y", [S, DM], F32, kind="ExternalOutput").ap()

    xn_scr = nc.dram_tensor("xn_scr", [NT, 16, 128, 512], BF16, kind="Internal").ap()
    qf_scr = nc.dram_tensor("qf_scr", [NT, 16, 128, 512], BF16, kind="Internal").ap()
    kf_scr = nc.dram_tensor("kf_scr", [NT, 16, 128, 512], BF16, kind="Internal").ap()
    vt_scr = nc.dram_tensor("vt_scr", [NT, 4, 128, 2048], BF16, kind="Internal").ap()
    av_scr = nc.dram_tensor("av_scr", [NT, 16, 128, 512], BF16, kind="Internal").ap()
    y1_scr = nc.dram_tensor("y1_scr", [NT, 4, 128, 2048], F32, kind="Internal").ap()
    y1f_scr = nc.dram_tensor("y1f_scr", [16, NT, 128, 512], BF16, kind="Internal").ap()
    wgt_scr = nc.dram_tensor("wgt_scr", [NJ, 16, 128, 256], BF16, kind="Internal").ap()
    wpt_scr = nc.dram_tensor("wpt_scr", [16, 128, 8192], BF16, kind="Internal").ap()
    h_scr = nc.dram_tensor("h_scr", [NT, NJ, 128, 512], BF16, kind="Internal").ap()
    _scr_map = dict(xn=xn_scr, qf=qf_scr, kf=kf_scr, vt=vt_scr, av=av_scr,
                    y1=y1_scr, y1f=y1f_scr, wgt=wgt_scr, wpt=wpt_scr, h=h_scr)
    _dbg_outs = {}
    for name in debug:
        t = _scr_map[name]
        _dbg_outs[name] = nc.dram_tensor(
            "dbg_" + name, list(t.shape), t.dtype, kind="ExternalOutput").ap()

    stack = ExitStack()
    E = stack.enter_context
    XN = E(nc.sbuf_tensor("XN", [128, 8192], BF16))
    XN2 = E(nc.sbuf_tensor("XN2", [128, 8192], BF16))
    WRES = E(nc.sbuf_tensor("WRES", [128, 32768], BF16))
    WTB = E(nc.sbuf_tensor("WTB", [128, 8192], BF16))
    PBTS = E(nc.sbuf_tensor("PBTS", [128, 2048], BF16))
    SB1 = E(nc.sbuf_tensor("SB1", [128, 4096], BF16))
    XF = E(nc.sbuf_tensor("XF", [128, 8192], F32))
    XF2 = E(nc.sbuf_tensor("XF2", [128, 7168], F32))
    SMF = E(nc.sbuf_tensor("SMF", [128, 512], F32))
    SMB = E(nc.sbuf_tensor("SMB", [128, 128], BF16))
    SROW = E(nc.sbuf_tensor("SROW", [1, 1024], F32))
    GBC = E(nc.sbuf_tensor("GBC", [128, 2048], F32))
    BBC = E(nc.sbuf_tensor("BBC", [128, 2048], F32))
    IDE = E(nc.sbuf_tensor("IDE", [128, 128], BF16))
    NEGM = E(nc.sbuf_tensor("NEGM", [128, 128], F32))
    ONESB = E(nc.sbuf_tensor("ONESB", [128, 1], BF16))
    ONEF = E(nc.sbuf_tensor("ONEF", [1, 1], F32))
    EPSB = E(nc.sbuf_tensor("EPSB", [128, 1], F32))
    P_MAIN = E(nc.psum_tensor("P_MAIN", [128, 2048], F32))
    P_AUX = E(nc.psum_tensor("P_AUX", [128, 1024], F32))
    P_BF = E(nc.psum_tensor("P_BF", [128, 2048], BF16))

    ctx = Ctx(nc)
    for sname in ["t", "v", "s", "g",
                  "dc", "dx", "dwld", "dxn_w", "dxn_r", "dqkv_w", "dqkv_r",
                  "dav_w", "dav_r", "dxb", "dy1_w", "dy1_r", "dy1f_w", "dy1f_r",
                  "dwg_ld", "dwgt_w", "dwgt_r", "dwpt_w", "dwpt_r",
                  "dh_w", "dh_r", "dout"]:
        ctx.newsem(stack, sname)

    cnt = ctx.cnt
    opq = ctx.op

    def smf(c0, n=1):
        return SMF[:, c0:c0 + n]

    opq("Y", lambda e: e.dma_start(out=IDE[:], in_=ident_in), inc="dc", incn=16)
    opq("Y", lambda e: e.dma_start(out=NEGM[:], in_=negm_in), inc="dc", incn=16)
    opq("Y", lambda e: e.dma_start(out=ONESB[:], in_=ones_in), inc="dc", incn=16)
    opq("Y", lambda e: e.dma_start(out=ONEF[:], in_=onef_in), inc="dc", incn=16)
    opq("S", lambda e: e.dma_start(out=GBC[:], in_=gbc_in), inc="dc", incn=16)
    opq("S", lambda e: e.dma_start(out=BBC[:], in_=bbc_in), inc="dc", incn=16)
    CONSTS = cnt["dc"]
    opq("V", lambda e: e.memset(EPSB[:], LNEPS), inc="v")
    EPS_V = cnt["v"]
    opq("G", lambda e: e.dma_start(
        out=WRES[:].rearrange("p (c o) -> p c o", c=16),
        in_=qkv_w[:, 0:2048].rearrange("(c p) o -> p c o", p=128)),
        inc="dwld", incn=16)

    xf_ring_guard = [None, None, None]
    stage_guard = [None, None]
    pbf_guard = [None, None]
    g_gamma_prev = None
    xns_t_guard = None

    for T in range(NT):
        stg = (XN, XN2)[T % 2]
        for s4 in range(4):
            sb = T * 4 + s4
            r = sb % 3
            opq("Y", (lambda e, _r=r, _sb=sb: e.dma_start(
                out=XF[:, _r * 2048:(_r + 1) * 2048],
                in_=x_in[_sb * 128:(_sb + 1) * 128, :])),
                deps=[("v", xf_ring_guard[r]) if xf_ring_guard[r] else None],
                inc="dx", incn=16)
            dxv = cnt["dx"]
            xs = XF[:, r * 2048:(r + 1) * 2048]
            for i in range(4):
                opq("V", (lambda e, _i=i, _xs=xs: e.bn_stats(
                    SMF[:, 6 * _i:6 * _i + 6], _xs[:, 512 * _i:512 * (_i + 1)])),
                    deps=[("dx", dxv)], inc="v")
            opq("V", lambda e: e.bn_aggr(smf(24, 2), smf(0, 24)),
                deps=[("v", cnt["v"]), ("s", cnt["s"])], inc="v")
            opq("S", lambda e: e.activation(smf(26), smf(25), AF.Sqrt, bias=EPSB[:]),
                deps=[("v", cnt["v"])], inc="s")
            opq("V", lambda e: e.reciprocal(smf(27), smf(26)),
                deps=[("s", cnt["s"])], inc="v")
            opq("V", (lambda e, _xs=xs: e.tensor_scalar(
                XF[:, 6144:8192], _xs, smf(24), smf(27), ALU.subtract, ALU.mult)),
                deps=[("v", cnt["v"]),
                      ("g", g_gamma_prev) if g_gamma_prev else None],
                inc="v")
            xf_ring_guard[r] = cnt["v"]
            opq("G", lambda e: e.tensor_tensor(
                XF2[:, 0:2048], XF[:, 6144:8192], GBC[:], ALU.mult),
                deps=[("v", cnt["v"]), ("dc", CONSTS)], inc="g")
            g_gamma_prev = cnt["g"]
            opq("G", lambda e: e.tensor_tensor(
                SB1[:, 0:2048], XF2[:, 0:2048], BBC[:], ALU.add),
                deps=[("g", cnt["g"]),
                      ("t", xns_t_guard) if xns_t_guard else None],
                inc="g")
            gxn = cnt["g"]
            for b in range(2):
                ph = pbf_guard[b]

                def tr_batch(e, _b=b):
                    last = None
                    for c in range(8):
                        cc = _b * 8 + c
                        last = e.transpose(
                            P_BF[:, _b * 1024 + c * 128:_b * 1024 + (c + 1) * 128],
                            SB1[:, cc * 128:(cc + 1) * 128], IDE[:])
                    return last
                opq("T", tr_batch,
                    deps=[("g", gxn), ("dc", CONSTS),
                          ("v", ph) if ph else None],
                    inc="t")
                tb = cnt["t"]
                sgd = stage_guard[T % 2] if (s4 == 0 and b == 0) else None
                opq("V", (lambda e, _b=b, _s4=s4, _stg=stg: e.tensor_copy(
                    _stg[:].rearrange("p (c f) -> p c f", c=16)[
                        :, _b * 8:(_b + 1) * 8, _s4 * 128:(_s4 + 1) * 128],
                    P_BF[:, _b * 1024:(_b + 1) * 1024].rearrange(
                        "p (c f) -> p c f", c=8))),
                    deps=[("t", tb), sgd], inc="v")
                pbf_guard[b] = cnt["v"]
            xns_t_guard = cnt["t"]
        opq("S", (lambda e, _T=T, _stg=stg: e.dma_start(
            out=xn_scr[_T].rearrange("c p f -> p c f"),
            in_=_stg[:].rearrange("p (c f) -> p c f", c=16))),
            deps=[("v", cnt["v"])], inc="dxn_w", incn=16)
        stage_guard[T % 2] = ("dxn_w", cnt["dxn_w"])
    AI_XN_W = cnt["dxn_w"]

    ctx.barrier()
    pslot_guard = [None] * 4
    wtb_guard = [None, None]
    xn_tile_guard = [None, None]
    wload_done = [cnt["dwld"], None, None]

    for pas, scr in ((0, qf_scr), (1, kf_scr), (2, vt_scr)):
        wload_subs = None
        if pas > 0:
            wload_subs = []
            for qw in range(4):
                opq("G", (lambda e, _cb=2048 * pas, _q=qw: e.dma_start(
                    out=WRES[:].rearrange("p (c o) -> p c o", c=16)[
                        :, :, _q * 512:(_q + 1) * 512],
                    in_=qkv_w[:, _cb + _q * 512:_cb + (_q + 1) * 512].rearrange(
                        "(c p) o -> p c o", p=128))),
                    deps=[("t", cnt["t"])], inc="dwld", incn=16)
                wload_subs.append(cnt["dwld"])
            wload_done[pas] = cnt["dwld"]
        for T in range(NT):
            xb = (XN, XN2)[T % 2]
            g = xn_tile_guard[T % 2]
            opq("Y", (lambda e, _T=T, _xb=xb: e.dma_start(
                out=_xb[:].rearrange("p (c f) -> p c f", c=16),
                in_=xn_scr[_T].rearrange("c p f -> p c f"))),
                deps=[("dxn_w", AI_XN_W),
                      ("t", g) if g else None],
                inc="dxn_r", incn=16)
            dxr = cnt["dxn_r"]
            half_last = [None, None]
            for grp in range(16):
                sl = grp % 4
                pg = pslot_guard[sl]
                if pas < 2:
                    def mmq(e, _g=grp, _sl=sl, _xb=xb):
                        last = None
                        for c in range(16):
                            last = e.matmul(
                                P_MAIN[:, _sl * 512:(_sl + 1) * 512],
                                WRES[:, c * 2048 + _g * 128:c * 2048 + (_g + 1) * 128],
                                _xb[:, c * 512:(c + 1) * 512],
                                start=(c == 0), stop=(c == 15))
                        return last
                else:
                    def mmq(e, _g=grp, _sl=sl, _xb=xb):
                        s4, fc = divmod(_g, 4)
                        last = None
                        for c in range(16):
                            last = e.matmul(
                                P_MAIN[:, _sl * 512:(_sl + 1) * 512],
                                _xb[:, c * 512 + s4 * 128:c * 512 + (s4 + 1) * 128],
                                WRES[:, c * 2048 + fc * 512:c * 2048 + (fc + 1) * 512],
                                start=(c == 0), stop=(c == 15))
                        return last
                wsub = (wload_done[pas] if wload_subs is None else
                        wload_subs[(grp // 4) if pas < 2 else (grp % 4)])
                opq("T", mmq,
                    deps=[("dxn_r", dxr), ("dwld", wsub),
                          ("v", pg) if pg else None],
                    inc="t")
                tmm = cnt["t"]
                half = grp // 8
                wg = wtb_guard[half]
                opq("V", (lambda e, _g=grp, _sl=sl, _h=half: e.tensor_copy(
                    WTB[:, _h * 4096 + (_g % 8) * 512:_h * 4096 + (_g % 8 + 1) * 512],
                    P_MAIN[:, _sl * 512:(_sl + 1) * 512])),
                    deps=[("t", tmm), wg], inc="v")
                pslot_guard[sl] = cnt["v"]
                half_last[half] = cnt["v"]
            xn_tile_guard[T % 2] = cnt["t"]
            for half in range(2):
                if pas < 2:
                    def store(e, _T=T, _h=half, _scr=scr):
                        return e.dma_start(
                            out=_scr[_T, _h * 8:(_h + 1) * 8].rearrange(
                                "c p f -> p c f"),
                            in_=WTB[:, _h * 4096:(_h + 1) * 4096].rearrange(
                                "p (c f) -> p c f", c=8))
                else:
                    def store(e, _T=T, _h=half, _scr=scr):
                        return e.dma_start(
                            out=_scr[_T, _h * 2:(_h + 1) * 2].rearrange(
                                "s p f -> p s f"),
                            in_=WTB[:, _h * 4096:(_h + 1) * 4096].rearrange(
                                "p (s f) -> p s f", s=2))
                opq("S", store, deps=[("v", half_last[half])],
                    inc="dqkv_w", incn=16)
                wtb_guard[half] = ("dqkv_w", cnt["dqkv_w"])
    AII_QKV_W = cnt["dqkv_w"]

    ctx.barrier()
    kv_guard = [None, None]
    qf_guard = None
    avf_store_guard = None
    psb_guard = [None, None]
    ptp_guard = [None, None]
    pts_guard = [None, None]
    pb_guard = [None, None]
    pmain_guard = None
    sb1_t_guard = None
    last_exp_s = None
    acc_guard = [None] * 4
    pending_fin = []
    fin_state = dict(pbf_av=None, sb1_t=None, avf_store=None)

    seq = 0
    for qb in range(NT):
        kv_ready = {}
        qf_ready_box = []

        def emit_kv_load(c):
            if c > qb:
                return
            kb = c % 2
            g = kv_guard[kb]
            opq("Y", (lambda e, _c=c, _kb=kb: e.dma_start(
                out=WRES[:, _kb * 8192:(_kb + 1) * 8192].rearrange(
                    "p (c f) -> p c f", c=16),
                in_=kf_scr[_c].rearrange("c p f -> p c f"))),
                deps=[("dqkv_w", AII_QKV_W), ("t", g) if g else None],
                inc="dqkv_r", incn=16)
            opq("Y", (lambda e, _c=c, _kb=kb: e.dma_start(
                out=WRES[:, 16384 + _kb * 8192:16384 + (_kb + 1) * 8192].rearrange(
                    "p (s f) -> p s f", s=4),
                in_=vt_scr[_c].rearrange("s p f -> p s f"))),
                deps=[("dqkv_w", AII_QKV_W)], inc="dqkv_r", incn=16)
            kv_ready[c] = cnt["dqkv_r"]
        emit_kv_load(0)
        emit_kv_load(1)
        opq("Y", (lambda e, _qb=qb: e.dma_start(
            out=XN[:].rearrange("p (c f) -> p c f", c=16),
            in_=qf_scr[_qb].rearrange("c p f -> p c f"))),
            deps=[("dqkv_w", AII_QKV_W), ("t", qf_guard) if qf_guard else None],
            inc="dqkv_r", incn=16)
        qf_ready = cnt["dqkv_r"]
        steps = [(c, t4) for c in range(qb + 1) for t4 in range(4)]
        nsteps = len(steps)

        def emit_qk(idx):
            c, t4 = steps[idx]
            kv_r = kv_ready[c]
            kb = c % 2
            KCH = WRES[:, kb * 8192:(kb + 1) * 8192]
            w = 512 if c < qb else (t4 + 1) * 128
            psl = (seq0 + idx) % 2
            PS = P_AUX[:, psl * 512:psl * 512 + 512]
            pg = psb_guard[psl]

            def qk(e, _t4=t4, _w=w, _PS=PS, _KCH=KCH):
                last = None
                for oc in range(16):
                    last = e.matmul(
                        _PS[:, 0:_w],
                        XN[:, oc * 512 + _t4 * 128:oc * 512 + (_t4 + 1) * 128],
                        _KCH[:, oc * 512:oc * 512 + _w],
                        start=(oc == 0), stop=(oc == 15))
                return last
            opq("T", qk,
                deps=[("dqkv_r", max(qf_ready, kv_r)),
                      ("s", pg) if pg else None],
                inc="t")
            return cnt["t"]

        seq0 = seq
        t_qk_pend = emit_qk(0)
        for idx in range(nsteps):
            c, t4 = steps[idx]
            if pending_fin and c == 0:
                pending_fin[0](t4)
                if t4 == 3:
                    pending_fin.pop(0)
            kb = c % 2
            VCH = WRES[:, 16384 + kb * 8192:16384 + (kb + 1) * 8192]
            w = 512 if c < qb else (t4 + 1) * 128
            nkt = w // 128
            psl = seq % 2
            PS = P_AUX[:, psl * 512:psl * 512 + 512]
            pbo = (0, 1536)[seq % 2]
            PB = PBTS[:, pbo:pbo + 512]
            pto = (512, 1024)[seq % 2]
            PTS = PBTS[:, pto:pto + 512]
            par_new = c % 2
            par_old = (c + 1) % 2
            Mold = smf(28 + t4 * 2 + par_old)
            Mnew = smf(28 + t4 * 2 + par_new)
            Lold = smf(36 + t4 * 2 + par_old)
            Lnew = smf(36 + t4 * 2 + par_new)
            t_qk = t_qk_pend
            vdeps = [("t", t_qk)]
            if c == qb:
                opq("V", (lambda e, _PS=PS, _w=w: e.tensor_tensor(
                    _PS[:, _w - 128:_w], _PS[:, _w - 128:_w], NEGM[:], ALU.add)),
                    deps=vdeps + [("dc", CONSTS)], inc="v")
                vdeps = [("v", cnt["v"])]
            opq("V", (lambda e, _PS=PS, _w=w: e.tensor_reduce(
                smf(44), _PS[:, 0:_w], AX.X, ALU.max)), deps=vdeps, inc="v")
            if c == 0:
                opq("V", (lambda e, _Mn=Mnew: e.tensor_scalar(
                    _Mn, smf(44), SSOFT, None, ALU.mult)),
                    deps=[("v", cnt["v"])], inc="v")
            else:
                opq("V", lambda e: e.tensor_scalar(
                    smf(45), smf(44), SSOFT, None, ALU.mult),
                    deps=[("v", cnt["v"])], inc="v")
                opq("V", (lambda e, _Mo=Mold, _Mn=Mnew: e.tensor_tensor(
                    _Mn, _Mo, smf(45), ALU.max)),
                    deps=[("v", cnt["v"])], inc="v")
            opq("V", (lambda e, _Mn=Mnew: e.tensor_scalar(
                smf(46), _Mn, -1.0, None, ALU.mult)),
                deps=[("v", cnt["v"]),
                      ("s", last_exp_s) if last_exp_s else None],
                inc="v")
            v_nb = cnt["v"]
            if c > 0:
                opq("S", (lambda e, _Mo=Mold: e.activation(
                    smf(47), _Mo, AF.Exp, bias=smf(46), scale=1.0)),
                    deps=[("v", v_nb)], inc="s")
            pbg = pb_guard[seq % 2]
            opq("S", (lambda e, _PS=PS, _PB=PB, _w=w: e.activation(
                _PB[:, 0:_w], _PS[:, 0:_w], AF.Exp, bias=smf(46), scale=SSOFT,
                accum_out=smf(48))),
                deps=[("v", v_nb), ("t", pbg) if pbg else None],
                inc="s")
            s_exp = cnt["s"]
            last_exp_s = s_exp
            psb_guard[psl] = s_exp
            if c == 0:
                opq("V", (lambda e, _Ln=Lnew: e.tensor_copy(_Ln, smf(48))),
                    deps=[("s", s_exp)], inc="v")
            else:
                opq("V", (lambda e, _Lo=Lold, _Ln=Lnew: e.tensor_scalar(
                    _Ln, _Lo, smf(47), smf(48), ALU.mult, ALU.add)),
                    deps=[("s", s_exp)], inc="v")
            acc = XF[:, t4 * 2048:(t4 + 1) * 2048]
            s_rescale = None
            if c > 0:
                opq("S", (lambda e, _acc=acc: e.activation(
                    _acc, _acc, AF.Identity, scale=smf(47))),
                    deps=[("v", acc_guard[t4]) if acc_guard[t4] else None,
                          ("s", cnt["s"])],
                    inc="s")
                s_rescale = cnt["s"]
            if idx + 1 < nsteps:
                t_qk_pend = emit_qk(idx + 1)
            ppg = ptp_guard[seq % 2]
            pt_base = (0, 512)[seq % 2]

            def ptr2(e, _PB=PB, _nkt=nkt, _ptb=pt_base):
                last = None
                for k in range(_nkt):
                    last = e.transpose(
                        P_BF[:, _ptb + k * 128:_ptb + (k + 1) * 128],
                        _PB[:, k * 128:(k + 1) * 128], IDE[:])
                return last
            opq("T", ptr2,
                deps=[("s", s_exp), ("v", ppg) if ppg else None],
                inc="t")
            pb_guard[seq % 2] = cnt["t"]
            t_ptr = cnt["t"]
            ptg = pts_guard[seq % 2]
            opq("V", (lambda e, _PTS=PTS, _nkt=nkt, _ptb=pt_base: e.tensor_copy(
                _PTS[:, 0:_nkt * 128], P_BF[:, _ptb:_ptb + _nkt * 128])),
                deps=[("t", t_ptr), ("t", ptg) if ptg else None],
                inc="v")
            v_pts = cnt["v"]
            ptp_guard[seq % 2] = v_pts

            def av(e, _PTS=PTS, _VCH=VCH, _nkt=nkt):
                last = None
                for fc in range(4):
                    for k in range(_nkt):
                        last = e.matmul(
                            P_MAIN[:, fc * 512:(fc + 1) * 512],
                            _PTS[:, k * 128:(k + 1) * 128],
                            _VCH[:, k * 2048 + fc * 512:k * 2048 + (fc + 1) * 512],
                            start=(k == 0), stop=(k == _nkt - 1))
                return last
            opq("T", av,
                deps=[("v", v_pts),
                      ("v", pmain_guard) if pmain_guard else None],
                inc="t")
            t_av = cnt["t"]
            pts_guard[seq % 2] = t_av
            if c == 0:
                opq("V", (lambda e, _acc=acc: e.tensor_copy(_acc, P_MAIN[:])),
                    deps=[("t", t_av)], inc="v")
            else:
                opq("V", (lambda e, _acc=acc: e.tensor_tensor(
                    _acc, _acc, P_MAIN[:], ALU.add)),
                    deps=[("t", t_av),
                          ("s", s_rescale) if s_rescale else None],
                    inc="v")
            pmain_guard = cnt["v"]
            acc_guard[t4] = cnt["v"]
            if t4 == 3:
                kv_guard[kb] = cnt["t"]
                emit_kv_load(c + 2)
            seq += 1
        def make_finalize(_qb):
            def fin_t4(t4):
                par = _qb % 2
                L = smf(36 + t4 * 2 + par)
                acc = XF[:, t4 * 2048:(t4 + 1) * 2048]
                opq("V", (lambda e, _L=L: e.reciprocal(smf(49), _L)),
                    deps=[("v", acc_guard[t4])], inc="v")
                opq("V", (lambda e, _acc=acc: e.tensor_scalar(
                    SB1[:, 0:2048], _acc, smf(49), None, ALU.mult)),
                    deps=[("v", cnt["v"]),
                          ("t", fin_state["sb1_t"]) if fin_state["sb1_t"] else None],
                    inc="v")
                v_avn = cnt["v"]
                for bb in range(2):
                    def trb(e, _b=bb):
                        last = None
                        for cc in range(8):
                            c16 = _b * 8 + cc
                            last = e.transpose(
                                P_BF[:, 1024 + cc * 128:1024 + (cc + 1) * 128],
                                SB1[:, c16 * 128:(c16 + 1) * 128], IDE[:])
                        return last
                    opq("T", trb,
                        deps=[("v", v_avn),
                              ("v", fin_state["pbf_av"]) if fin_state["pbf_av"] else None],
                        inc="t")
                    tb = cnt["t"]
                    sgd = fin_state["avf_store"] if (bb == 0 and t4 == 0) else None
                    opq("V", (lambda e, _b=bb, _t4=t4: e.tensor_copy(
                        XN2[:].rearrange("p (c f) -> p c f", c=16)[
                            :, _b * 8:(_b + 1) * 8, _t4 * 128:(_t4 + 1) * 128],
                        P_BF[:, 1024:2048].rearrange("p (c f) -> p c f", c=8))),
                        deps=[("t", tb), sgd], inc="v")
                    fin_state["pbf_av"] = cnt["v"]
                fin_state["sb1_t"] = cnt["t"]
                if t4 == 3:
                    opq("S", (lambda e, _q=_qb: e.dma_start(
                        out=av_scr[_q].rearrange("c p f -> p c f"),
                        in_=XN2[:].rearrange("p (c f) -> p c f", c=16))),
                        deps=[("v", fin_state["pbf_av"])], inc="dav_w", incn=16)
                    fin_state["avf_store"] = ("dav_w", cnt["dav_w"])
            return fin_t4
        pending_fin.append(make_finalize(qb))
        qf_guard = cnt["t"]

    while pending_fin:
        f = pending_fin.pop(0)
        for _t4 in range(4):
            f(_t4)
    B1_AV_W = cnt["dav_w"]

    ctx.barrier()
    opq("G", lambda e: e.dma_start(
        out=WRES[:].rearrange("p (c o) -> p c o", c=16),
        in_=o_w[:, :].rearrange("(c p) o -> p c o", p=128)),
        deps=[("t", cnt["t"])], inc="dwld", incn=16)
    WO_LD = cnt["dwld"]
    avf_guard = None
    gz1_prev = None
    xb_ring_guard = [None, None, None]
    y1f_stage_guard = None
    pslot_guard = [None] * 4
    pbf_b2_guard = None
    y1sb1_t_guard = None
    y1_store_prev = 0
    pending_trb = []
    b2st = dict(pbf=None, sb1_t=None, y1f_store=None)
    for T in range(NT):
        opq("Y", (lambda e, _T=T: e.dma_start(
            out=XN2[:].rearrange("p (c f) -> p c f", c=16),
            in_=av_scr[_T].rearrange("c p f -> p c f"))),
            deps=[("dav_w", B1_AV_W), ("t", avf_guard) if avf_guard else None],
            inc="dav_r", incn=16)
        davr = cnt["dav_r"]
        for s4 in range(4):
            sb = T * 4 + s4
            r = sb % 3
            opq("Y", (lambda e, _r=r, _sb=sb: e.dma_start(
                out=XF[:, _r * 2048:(_r + 1) * 2048],
                in_=x_in[_sb * 128:(_sb + 1) * 128, :])),
                deps=[("g", xb_ring_guard[r]) if xb_ring_guard[r] else None],
                inc="dxb", incn=16)
            dxb = cnt["dxb"]
            xs = XF[:, r * 2048:(r + 1) * 2048]
            for oc2 in range(4):
                sl = oc2
                pg = pslot_guard[sl]

                def omm(e, _s4=s4, _oc2=oc2, _sl=sl):
                    last = None
                    for cc in range(16):
                        last = e.matmul(
                            P_MAIN[:, _sl * 512:(_sl + 1) * 512],
                            XN2[:, cc * 512 + _s4 * 128:cc * 512 + (_s4 + 1) * 128],
                            WRES[:, cc * 2048 + _oc2 * 512:cc * 2048 + (_oc2 + 1) * 512],
                            start=(cc == 0), stop=(cc == 15))
                    return last
                opq("T", omm,
                    deps=[("dav_r", davr), ("dwld", WO_LD),
                          ("v", pg) if pg else None],
                    inc="t")
                tmm = cnt["t"]
                opq("V", (lambda e, _oc2=oc2, _sl=sl, _xs=xs: e.tensor_tensor(
                    XF2[:, _oc2 * 512:(_oc2 + 1) * 512],
                    P_MAIN[:, _sl * 512:(_sl + 1) * 512],
                    _xs[:, _oc2 * 512:(_oc2 + 1) * 512], ALU.add)),
                    deps=[("t", tmm), ("dxb", dxb),
                          ("g", gz1_prev) if gz1_prev else None], inc="v")
                pslot_guard[sl] = cnt["v"]
            while pending_trb:
                pending_trb.pop(0)()
            opq("G", (lambda e, _xs=xs: e.tensor_tensor(
                XF2[:, 2048:4096], XF2[:, 0:2048], _xs, ALU.add)),
                deps=[("v", cnt["v"]), ("g", cnt["g"])], inc="g")
            xb_ring_guard[r] = cnt["g"]
            gz1 = cnt["g"]
            gz1_prev = gz1
            for i in range(4):
                opq("V", (lambda e, _i=i: e.bn_stats(
                    SMF[:, 6 * _i:6 * _i + 6],
                    XF2[:, 2048 + 512 * _i:2048 + 512 * (_i + 1)])),
                    deps=[("g", gz1)], inc="v")
            opq("V", lambda e: e.bn_aggr(smf(24, 2), smf(0, 24)),
                deps=[("v", cnt["v"]), ("s", cnt["s"])], inc="v")
            opq("S", lambda e: e.activation(smf(26), smf(25), AF.Sqrt, bias=EPSB[:]),
                deps=[("v", cnt["v"])], inc="s")
            opq("V", lambda e: e.reciprocal(smf(27), smf(26)),
                deps=[("s", cnt["s"])], inc="v")
            opq("V", lambda e: e.tensor_scalar(
                XF2[:, 4096:6144], XF2[:, 2048:4096], smf(24), smf(27),
                ALU.subtract, ALU.mult),
                deps=[("v", cnt["v"]), ("dy1_w", y1_store_prev)], inc="v")
            v_y1 = cnt["v"]
            opq("S", (lambda e, _T=T, _s4=s4: e.dma_start(
                out=y1_scr[_T, _s4], in_=XF2[:, 4096:6144])),
                deps=[("v", v_y1)], inc="dy1_w", incn=16)
            y1_store_prev = cnt["dy1_w"]
            opq("V", lambda e: e.tensor_copy(SB1[:, 0:2048], XF2[:, 4096:6144]),
                deps=[("v", v_y1),
                      ("t", y1sb1_t_guard) if y1sb1_t_guard else None],
                inc="v")
            v_y1b = cnt["v"]

            def make_trb_part(_T=T, _s4=s4, _v_y1b=v_y1b):
                def run():
                    nonlocal_state = None
                    global pbf_b2_guard_ref
                    for b in range(2):
                        def trb(e, _b=b):
                            last = None
                            for cc in range(8):
                                c16 = _b * 8 + cc
                                last = e.transpose(
                                    P_BF[:, cc * 128:(cc + 1) * 128],
                                    SB1[:, c16 * 128:(c16 + 1) * 128], IDE[:])
                            return last
                        opq("T", trb,
                            deps=[("v", _v_y1b),
                                  ("v", b2st["pbf"]) if b2st["pbf"] else None],
                            inc="t")
                        tb = cnt["t"]
                        sgd = b2st["y1f_store"] if (b == 0 and _s4 == 0) else None
                        opq("V", (lambda e, _b=b, _ss=_s4: e.tensor_copy(
                            XN[:].rearrange("p (c f) -> p c f", c=16)[
                                :, _b * 8:(_b + 1) * 8, _ss * 128:(_ss + 1) * 128],
                            P_BF[:, 0:1024].rearrange("p (c f) -> p c f", c=8))),
                            deps=[("t", tb), sgd], inc="v")
                        b2st["pbf"] = cnt["v"]
                    b2st["sb1_t"] = cnt["t"]
                    if _s4 == 3:
                        opq("S", (lambda e, _TT=_T: e.dma_start(
                            out=y1f_scr[:, _TT].rearrange("c p f -> p c f"),
                            in_=XN[:].rearrange("p (c f) -> p c f", c=16))),
                            deps=[("v", b2st["pbf"])], inc="dy1f_w", incn=16)
                        b2st["y1f_store"] = ("dy1f_w", cnt["dy1f_w"])
                return run
            pending_trb.append(make_trb_part())
            y1sb1_t_guard = b2st["sb1_t"]
        avf_guard = cnt["t"]
    while pending_trb:
        pending_trb.pop(0)()
    B2_Y1F_W = cnt["dy1f_w"]
    B2_Y1_W = cnt["dy1_w"]

    wq = dict(vt=[None, None], wh_t=[None, None], whg=[None, None],
              pbfw=None, stage_guard=None, clipg=[None, None])

    def w_stage1(t):
        par = t % 2
        cb = XN[:, par * 2048:(par + 1) * 2048]
        tq_ap = XN[:, 4096 + par * 2048:4096 + (par + 1) * 2048]
        qf = XF[:, par * 2048:(par + 1) * 2048]
        qc = XF[:, 4096 + par * 2048:4096 + (par + 1) * 2048]
        wh = XN2[:, 4096 + par * 2048:4096 + (par + 1) * 2048]
        red = SMF[:, 119 + par * 32:119 + par * 32 + 32]
        rcp = SMF[:, 183 + par * 32:183 + par * 32 + 32]
        sbf = SMB[:, par * 32:par * 32 + 32]
        if t < 128:
            p, half = divmod(t, 2)
            rb = half * HF + p * 128

            def ld(e, _rb=rb, _cb=cb):
                return e.dma_start(out=_cb, in_=wgu_in[_rb:_rb + 128, :])
        else:
            i, q = divmod(t - 128, 4)

            def ld(e, _i=i, _q=q, _cb=cb):
                return e.dma_start(
                    out=_cb,
                    in_=wp_in[_i * 128:(_i + 1) * 128, _q * 2048:(_q + 1) * 2048])
        opq("G", ld, deps=[("v", wq["vt"][par]) if wq["vt"][par] else None],
            inc="dwg_ld", incn=16)
        dld = cnt["dwg_ld"]
        opq("V", (lambda e, _cb=cb, _red=red: e.tensor_reduce(
            _red, _cb.rearrange("p (g k) -> p g k", k=64),
            AX.X, ALU.add, apply_absolute_value=True)),
            deps=[("dwg_ld", dld)], inc="v")
        opq("V", (lambda e, _red=red, _sbf=sbf: e.tensor_scalar(
            _sbf, _red, 1.0 / 64.0, 1e-8, ALU.mult, ALU.max)),
            deps=[("v", cnt["v"]),
                  ("g", wq["whg"][par]) if wq["whg"][par] else None],
            inc="v")
        opq("V", (lambda e, _sbf=sbf, _rcp=rcp: e.reciprocal(_rcp, _sbf)),
            deps=[("v", cnt["v"])], inc="v")
        opq("V", (lambda e, _cb=cb, _rcp=rcp, _tq=tq_ap: e.tensor_tensor(
            _tq.rearrange("p (g k) -> p g k", k=64),
            _cb.rearrange("p (g k) -> p g k", k=64),
            _rcp.unsqueeze(2).broadcast_to((128, 32, 64)), ALU.mult)),
            deps=[("v", cnt["v"])], inc="v")
        wq["vt"][par] = cnt["v"]
        opq("V", (lambda e, _tq=tq_ap, _qf=qf: e.tensor_scalar(
            _qf, _tq, MAGIC, -MAGIC, ALU.add, ALU.add)),
            deps=[("v", cnt["v"]),
                  ("g", wq["clipg"][par]) if wq["clipg"][par] else None],
            inc="v")
        v_qf = cnt["v"]
        opq("G", (lambda e, _qf=qf, _qc=qc: e.tensor_scalar(
            _qc, _qf, 1.0, -1.0, ALU.min, ALU.max)),
            deps=[("v", v_qf)], inc="g")
        wq["clipg"][par] = cnt["g"]
        opq("G", (lambda e, _qc=qc, _sbf=sbf, _wh=wh: e.tensor_tensor(
            _wh.rearrange("p (g k) -> p g k", k=64),
            _qc.rearrange("p (g k) -> p g k", k=64),
            _sbf.unsqueeze(2).broadcast_to((128, 32, 64)), ALU.mult)),
            deps=[("g", cnt["g"]),
                  ("t", wq["wh_t"][par]) if wq["wh_t"][par] else None],
            inc="g")
        wq["whg"][par] = cnt["g"]

    def w_stage2(t):
        par = t % 2
        g_wh = wq["whg"][par]
        last_v = None
        for b in range(2):
            pg = wq["pbfw"]

            def trb(e, _b=b, _par=par):
                last = None
                for cc in range(8):
                    c16 = _b * 8 + cc
                    last = e.transpose(
                        P_BF[:, cc * 128:(cc + 1) * 128],
                        XN2[:, 4096 + _par * 2048 + c16 * 128:
                            4096 + _par * 2048 + (c16 + 1) * 128], IDE[:])
                return last
            opq("T", trb, deps=[("g", g_wh), ("v", pg) if pg else None],
                inc="t")
            tb = cnt["t"]
            sgd = wq["stage_guard"] if b == 0 else None
            if t < 128:
                half = t % 2

                def ev(e, _b=b, _h=half):
                    return e.tensor_copy(
                        XN2[:, 0:4096].rearrange("p (c f) -> p c f", c=16)[
                            :, _b * 8:(_b + 1) * 8, _h * 128:(_h + 1) * 128],
                        P_BF[:, 0:1024].rearrange("p (c f) -> p c f", c=8))
            else:
                st = par

                def ev(e, _b=b, _st=st):
                    return e.tensor_copy(
                        XN2[:, _st * 2048:(_st + 1) * 2048].rearrange(
                            "p (j f) -> p j f", j=16)[:, _b * 8:(_b + 1) * 8, :],
                        P_BF[:, 0:1024].rearrange("p (c f) -> p c f", c=8))
            opq("V", ev, deps=[("t", tb), sgd], inc="v")
            wq["pbfw"] = cnt["v"]
            last_v = cnt["v"]
        wq["wh_t"][par] = cnt["t"]
        if t < 128:
            if t % 2 == 1:
                p = t // 2
                opq("S", (lambda e, _j=p: e.dma_start(
                    out=wgt_scr[_j].rearrange("c p f -> p c f"),
                    in_=XN2[:, 0:4096].rearrange("p (c f) -> p c f", c=16))),
                    deps=[("v", last_v)], inc="dwgt_w", incn=16)
                wq["stage_guard"] = ("dwgt_w", cnt["dwgt_w"])
        else:
            i, q = divmod(t - 128, 4)
            st = par
            opq("S", (lambda e, _i=i, _q=q, _st=st: e.dma_start(
                out=wpt_scr[_i, :, _q * 2048:(_q + 1) * 2048],
                in_=XN2[:, _st * 2048:(_st + 1) * 2048])),
                deps=[("v", last_v)], inc="dwpt_w", incn=16)
            wq["stage_guard"] = ("dwpt_w", cnt["dwpt_w"])

    NTASK = 192
    w_s1 = [0]
    w_s2 = [0]

    def w_advance(n1, n2):
        for _ in range(n2):
            if w_s2[0] < min(w_s1[0], NTASK):
                w_stage2(w_s2[0])
                w_s2[0] += 1
        for _ in range(n1):
            if w_s1[0] < NTASK:
                w_stage1(w_s1[0])
                w_s1[0] += 1

    ctx.barrier(extra=[("dy1f_w", B2_Y1F_W), ("dqkv_w", AII_QKV_W),
                       ("dy1_w", B2_Y1_W)])
    wtb_pair_guard = [None, None]
    hc_store_guard = [None, None]
    silu_v_guard = [None, None]
    pmain_c_guard = [None] * 3
    w_advance(2, 0)
    for _ in range(4):
        w_advance(2, 2)
    for half in range(2):
        for tloc in range(4):
            opq("Y", (lambda e, _h=half, _tl=tloc: e.dma_start(
                out=WRES[:].rearrange("p (c f) -> p c f", c=16)[
                    :, :, _tl * 512:(_tl + 1) * 512],
                in_=y1f_scr[:, _h * 4 + _tl].rearrange("c p f -> p c f"))),
                deps=[("dy1f_w", B2_Y1F_W), ("t", cnt["t"])],
                inc="dy1f_r", incn=16)
        y1f_ready = cnt["dy1f_r"]
        for j in range(NJ):
            wb = j % 2
            pg = wtb_pair_guard[wb]
            opq("Y", (lambda e, _j=j, _wb=wb: e.dma_start(
                out=WTB[:, _wb * 4096:(_wb + 1) * 4096].rearrange(
                    "p (c f) -> p c f", c=16),
                in_=wgt_scr[_j].rearrange("c p f -> p c f"))),
                deps=[("dwgt_w", 16 * (_jj := j + 1)), ("t", pg) if pg else None],
                inc="dwgt_r", incn=16)
            dwr = cnt["dwgt_r"]
            WG = WTB[:, wb * 4096:(wb + 1) * 4096]
            hcs = (j % 2) * 2048
            for th in range(4):
                sisl = ((j * 4 + th) % 2) * 512
                pr = (j * 4 + th) % 3
                if pr < 2:
                    PG_ = P_MAIN[:, pr * 1024:pr * 1024 + 512]
                    PU_ = P_MAIN[:, pr * 1024 + 512:pr * 1024 + 1024]
                else:
                    PG_ = P_AUX[:, 0:512]
                    PU_ = P_AUX[:, 512:1024]
                pgm = pmain_c_guard[pr]

                def gmm(e, _th=th, _WG=WG, _PG=PG_):
                    last = None
                    for cc in range(16):
                        last = e.matmul(
                            _PG,
                            _WG[:, cc * 256:cc * 256 + 128],
                            WRES[:, cc * 2048 + _th * 512:cc * 2048 + (_th + 1) * 512],
                            start=(cc == 0), stop=(cc == 15))
                    return last
                opq("T", gmm,
                    deps=[("dwgt_r", dwr), ("dy1f_r", y1f_ready),
                          ("v", pgm) if pgm else None],
                    inc="t")
                t_g = cnt["t"]

                def umm(e, _th=th, _WG=WG, _PU=PU_):
                    last = None
                    for cc in range(16):
                        last = e.matmul(
                            _PU,
                            _WG[:, cc * 256 + 128:cc * 256 + 256],
                            WRES[:, cc * 2048 + _th * 512:cc * 2048 + (_th + 1) * 512],
                            start=(cc == 0), stop=(cc == 15))
                    return last
                opq("T", umm, deps=[], inc="t")
                t_u = cnt["t"]
                opq("S", (lambda e, _PG=PG_, _ss=sisl: e.activation(
                    XF2[:, _ss:_ss + 512], _PG, AF.Silu)),
                    deps=[("t", t_g), ("s", cnt["s"]),
                          ("v", silu_v_guard[(j * 4 + th) % 2])
                          if silu_v_guard[(j * 4 + th) % 2] else None],
                    inc="s")
                s_si = cnt["s"]
                opq("V", (lambda e, _th=th, _PU=PU_, _hc=hcs, _ss=sisl:
                          e.tensor_tensor(
                    SB1[:, _hc + _th * 512:_hc + (_th + 1) * 512],
                    XF2[:, _ss:_ss + 512],
                    _PU, ALU.mult)),
                    deps=[("s", s_si), ("t", t_u), hc_store_guard[j % 2]],
                    inc="v")
                silu_v_guard[(j * 4 + th) % 2] = cnt["v"]
                pmain_c_guard[pr] = cnt["v"]
            wtb_pair_guard[wb] = cnt["t"]
            opq("S", (lambda e, _h=half, _j=j, _hc=hcs: e.dma_start(
                out=h_scr[_h * 4:(_h + 1) * 4, _j].rearrange("t p f -> p t f"),
                in_=SB1[:, _hc:_hc + 2048].rearrange("p (t f) -> p t f", t=4))),
                deps=[("v", cnt["v"])], inc="dh_w", incn=16)
            hc_store_guard[j % 2] = ("dh_w", cnt["dh_w"])
            w_advance(2, 2)
    while w_s2[0] < NTASK:
        w_advance(2, 2)
    C_H_W = cnt["dh_w"]
    WPT_W = cnt["dwpt_w"]

    ctx.barrier(extra=[("dh_w", C_H_W), ("dwpt_w", WPT_W),
                       ("dwgt_w", cnt["dwgt_w"]), ("dy1_w", B2_Y1_W)])
    hq_guard = None
    wpt_buf_guard = [None, None]
    out2f_guard = None
    y1_ring_guard = [None, None, None]
    pm_d_guard = [None] * 4
    pbf_d_guard = [None, None]
    paux_ss_guard = None
    paux_tr_guard = None
    z_slot_guard = [0, 0, 0, 0]
    for tq in range(NT):
        opq("Y", (lambda e, _tq=tq: e.dma_start(
            out=WRES[:].rearrange("p (j f) -> p j f", j=64),
            in_=h_scr[_tq].rearrange("j p f -> p j f"))),
            deps=[("dh_w", C_H_W), ("t", hq_guard[0]) if hq_guard else None,
                  ("s", hq_guard[1]) if hq_guard else None],
            inc="dh_r", incn=16)
        dhq = cnt["dh_r"]
        for dc in range(16):
            wb = dc % 2
            buf = (XN, WTB)[wb]
            g = wpt_buf_guard[wb]
            opq("Y", (lambda e, _dc=dc, _buf=buf: e.dma_start(
                out=_buf[:], in_=wpt_scr[_dc])),
                deps=[("dwpt_w", WPT_W), ("t", g) if g else None],
                inc="dwpt_r", incn=16)
            dwp = cnt["dwpt_r"]
            sl = dc % 4
            pg = pm_d_guard[sl]

            def pmm(e, _buf=buf, _sl=sl):
                last = None
                for j in range(64):
                    last = e.matmul(P_MAIN[:, _sl * 512:(_sl + 1) * 512],
                                    _buf[:, j * 128:(j + 1) * 128],
                                    WRES[:, j * 512:(j + 1) * 512],
                                    start=(j == 0), stop=(j == 63))
                return last
            opq("T", pmm,
                deps=[("dwpt_r", dwp), ("dh_r", dhq),
                      ("v", pg) if pg else None],
                inc="t")
            wpt_buf_guard[wb] = cnt["t"]
            t_p = cnt["t"]
            sgd = ("t", out2f_guard) if (out2f_guard and dc == 0) else None
            opq("V", (lambda e, _dc=dc, _sl=sl: e.tensor_copy(
                XN2[:, _dc * 512:(_dc + 1) * 512],
                P_MAIN[:, _sl * 512:(_sl + 1) * 512])),
                deps=[("t", t_p), sgd], inc="v")
            pm_d_guard[sl] = cnt["v"]
            for bb in ((dc * 2, dc * 2 + 1) if dc < 8 else ()):
                opq("S", (lambda e, _bb=bb: e.activation(
                    SB1[:, 0:2048], WRES[:, _bb * 2048:(_bb + 1) * 2048],
                    AF.Square)),
                    deps=[("dh_r", dhq), ("t", cnt["t"]), ("s", cnt["s"])],
                    inc="s")
                s_sq = cnt["s"]

                def ssmm(e, _bb=bb):
                    last = None
                    for k in range(4):
                        last = e.matmul(P_AUX[0:1, 0:512], ONESB[:],
                                        SB1[:, k * 512:(k + 1) * 512],
                                        start=(_bb == 0 and k == 0),
                                        stop=(_bb == 15 and k == 3))
                    return last
                opq("T", ssmm,
                    deps=[("s", s_sq), ("dc", CONSTS),
                          ("v", paux_ss_guard) if (_bb_is0 := (bb == 0)) and paux_ss_guard
                          else None],
                    inc="t")
                if bb == 15:
                    wres_sq_done = cnt["s"]
        t_ss = cnt["t"]
        wres_free = (t_ss, wres_sq_done)
        opq("V", lambda e: e.tensor_scalar(
            SROW[0:1, 0:512], P_AUX[0:1, 0:512], 1.0 / 8192.0, RMSEPS,
            ALU.mult, ALU.add),
            deps=[("t", t_ss)], inc="v")
        paux_ss_guard = cnt["v"]
        opq("S", lambda e: e.activation(SROW[0:1, 512:1024], SROW[0:1, 0:512],
                                        AF.Sqrt),
            deps=[("v", cnt["v"])], inc="s")
        opq("V", lambda e: e.reciprocal(SROW[0:1, 0:512], SROW[0:1, 512:1024]),
            deps=[("s", cnt["s"])], inc="v")
        v_rstd = cnt["v"]

        def rtr(e):
            last = None
            for k in range(4):
                last = e.transpose(P_AUX[:, 512 + k:512 + k + 1],
                                   SROW[0:1, k * 128:(k + 1) * 128],
                                   ONEF[:])
            return last
        opq("T", rtr,
            deps=[("v", v_rstd),
                  ("v", paux_tr_guard) if paux_tr_guard else None],
            inc="t")
        opq("V", lambda e: e.tensor_copy(smf(50, 4), P_AUX[:, 512:516]),
            deps=[("t", cnt["t"])], inc="v")
        v_rstdt = cnt["v"]
        paux_tr_guard = v_rstdt
        v_out2 = cnt["v"]
        ttr_v = []
        for s4 in range(4):
            for hh in range(2):
                def ttrh(e, _s4=s4, _hh=hh):
                    last = None
                    for dcl in range(8):
                        dc = _hh * 8 + dcl
                        last = e.transpose(
                            P_BF[:, _hh * 1024 + dcl * 128:_hh * 1024 + (dcl + 1) * 128],
                            XN2[:, dc * 512 + _s4 * 128:dc * 512 + (_s4 + 1) * 128],
                            IDE[:])
                    return last
                opq("T", ttrh,
                    deps=[("v", v_out2),
                          ("v", pbf_d_guard[hh]) if pbf_d_guard[hh] else None],
                    inc="t")
                t_tm = cnt["t"]
                opq("V", (lambda e, _s4=s4, _hh=hh: e.tensor_scalar(
                    XF[:, _s4 * 2048 + _hh * 1024:_s4 * 2048 + (_hh + 1) * 1024],
                    P_BF[:, _hh * 1024:(_hh + 1) * 1024],
                    smf(50 + _s4), None, ALU.mult)),
                    deps=[("t", t_tm), ("v", v_rstdt),
                          ("dout", z_slot_guard[s4])],
                    inc="v")
                pbf_d_guard[hh] = cnt["v"]
            ttr_v.append(cnt["v"])
        for s4 in range(4):
            v_z = ttr_v[s4]
            rr = s4 % 3
            g = y1_ring_guard[rr]
            opq("G", (lambda e, _tq=tq, _s4=s4, _rr=rr: e.dma_start(
                out=XF2[:, _rr * 2048:(_rr + 1) * 2048],
                in_=y1_scr[_tq, _s4])),
                deps=[("dy1_w", B2_Y1_W), ("g", g) if g else None],
                inc="dy1_r", incn=16)
            dy1 = cnt["dy1_r"]
            Z = XF[:, s4 * 2048:(s4 + 1) * 2048]
            opq("G", (lambda e, _Z=Z, _rr=rr: e.tensor_tensor(
                _Z, _Z, XF2[:, _rr * 2048:(_rr + 1) * 2048], ALU.add)),
                deps=[("v", v_z), ("dy1_r", dy1), ("g", cnt["g"])], inc="g")
            y1_ring_guard[rr] = cnt["g"]
            gz2 = cnt["g"]
            for i in range(4):
                opq("V", (lambda e, _i=i, _Z=Z: e.bn_stats(
                    SMF[:, 6 * _i:6 * _i + 6],
                    _Z[:, 512 * _i:512 * (_i + 1)])),
                    deps=[("g", gz2)], inc="v")
            opq("V", lambda e: e.bn_aggr(smf(24, 2), smf(0, 24)),
                deps=[("v", cnt["v"]), ("s", cnt["s"])], inc="v")
            opq("S", lambda e: e.activation(smf(26), smf(25), AF.Sqrt, bias=EPSB[:]),
                deps=[("v", cnt["v"])], inc="s")
            opq("V", lambda e: e.reciprocal(smf(27), smf(26)),
                deps=[("s", cnt["s"])], inc="v")
            opq("V", (lambda e, _Z=Z: e.tensor_scalar(
                _Z, _Z, smf(24), smf(27), ALU.subtract, ALU.mult)),
                deps=[("v", cnt["v"])], inc="v")
            v_fin = cnt["v"]
            opq("S", (lambda e, _tq=tq, _s4=s4, _Z=Z: e.dma_start(
                out=y_out[(_tq * 4 + _s4) * 128:(_tq * 4 + _s4 + 1) * 128, :],
                in_=_Z)),
                deps=[("v", v_fin)], inc="dout", incn=16)
            z_slot_guard[s4] = cnt["dout"]
        out2f_guard = cnt["t"]
        hq_guard = wres_free
    for name, dst in _dbg_outs.items():
        opq("Y", (lambda e, _d=dst, _s=_scr_map[name]: e.dma_start(
            out=_d, in_=_s)),
            deps=[("dout", cnt["dout"]), ("t", cnt["t"]), ("v", cnt["v"]),
                  ("s", cnt["s"]), ("g", cnt["g"])],
            inc="dout", incn=16)
    opq("Y", lambda e: None, deps=[("dout", cnt["dout"])])

    with stack:
        with nc.Block() as block:
            for eng, dec in (("T", block.tensor), ("V", block.vector),
                             ("S", block.scalar), ("G", block.gpsimd),
                             ("Y", block.sync)):
                fl = [f for en, f in ctx.ops if en == eng]

                def make(fl_):
                    def runner(e):
                        for f in fl_:
                            f(e)
                    return runner
                dec(make(fl))
    return nc


_NC_CACHE = {}


def kernel(L0_attn_x, L0_attn_qkv, L0_attn_o_proj, L0_attn_gamma, L0_attn_beta,
           L0_ffn_w_gate_up, L0_ffn_w_proj):
    x = np.ascontiguousarray(np.asarray(L0_attn_x, dtype=np.float32))
    qkv = np.ascontiguousarray(np.asarray(L0_attn_qkv, dtype=np.float32))
    o_w = np.ascontiguousarray(np.asarray(L0_attn_o_proj, dtype=np.float32))
    gamma = np.asarray(L0_attn_gamma, dtype=np.float32)
    beta = np.asarray(L0_attn_beta, dtype=np.float32)
    wgu = np.ascontiguousarray(np.asarray(L0_ffn_w_gate_up, dtype=np.float32))
    wp = np.ascontiguousarray(np.asarray(L0_ffn_w_proj, dtype=np.float32))

    gbc = np.ascontiguousarray(np.broadcast_to(gamma[None, :], (128, DM)))
    bbc = np.ascontiguousarray(np.broadcast_to(beta[None, :], (128, DM)))
    ident = np.eye(128, dtype=BF)
    negm = np.where(np.arange(128)[None, :] <= np.arange(128)[:, None],
                    np.float32(0.0), np.float32(-1e9)).astype(np.float32)
    onesb = np.ones((128, 1), dtype=BF)

    if "nc" not in _NC_CACHE:
        _NC_CACHE["nc"] = build_program()
    nc = _NC_CACHE["nc"]

    in_maps = []
    for b in range(8):
        in_maps.append(dict(x=np.ascontiguousarray(x[b]), qkv=qkv, o_w=o_w,
                            gbc=gbc, bbc=bbc, wgu=wgu, wp=wp,
                            ident=ident, negm=negm, onesb=onesb,
                            onef=np.ones((1, 1), dtype=np.float32)))
    res = run_bass_kernel_spmd(nc, in_maps, core_ids=list(range(8)))
    out = np.stack([np.asarray(res.results[b]["y"], dtype=np.float32)
                    for b in range(8)], axis=0)
    return out


